# version 24
# baseline (speedup 1.0000x reference)
import contextlib
import numpy as np
import concourse.bass as bass
import concourse.mybir as mybir
from concourse.bass_utils import run_bass_kernel_spmd

F32 = mybir.dt.float32
BF16 = mybir.dt.bfloat16
I32 = mybir.dt.int32
AOT = mybir.AluOpType
AF = mybir.ActivationFunctionType
AX = mybir.AxisListType

MAGIC_HALF = 96.0
MAGIC_ONE = 192.0
RSQRT_MAGIC = 0x5f3759df
QW = 1024

QUANT_TAGS = 17
QBF_W = False
QBF_X = False
QBF_H = True
RSQRT_TAGS = 13
LN_TAGS = 2 + 3 + RSQRT_TAGS + 1


def full_cfg():
    return dict(S=4096, E=1024, H=16, D=64, I=2816)


class Sem:
    def __init__(self, nc, name):
        self.h = nc.alloc_semaphore(name)
        self.n = 0

    def tag(self, instr, k=1):
        instr.then_inc(self.h, k)
        self.n += k
        return self.n

    def wait(self, eng, v=None):
        eng.wait_ge(self.h, self.n if v is None else v)


def quant_calls(W):
    calls = []
    off = 0
    while off < W:
        calls.append((off, min(QW, W - off)))
        off += QW
    return calls


def emit_quant(v, src, out_bf, b1, b2, b3, b4, s_ap, rs_t, W, vs, bf=False):
    NB = W // 16
    MH, MO = (96.0, 192.0) if bf else (6291456.0, 12582912.0)
    sv = lambda t: t.unsqueeze(2).broadcast_to([128, NB, 16])
    blk = lambda t: t.rearrange('p (nb b) -> p nb b', b=16)
    vs.tag(v.reduce_max(rs_t[:, 0:NB], blk(src[:, 0:W]), axis=AX.X, apply_absolute_value=True))
    vs.wait(v)
    vs.tag(v.tensor_scalar(rs_t[:, 0:NB], rs_t[:, 0:NB], 1.0 / 6.0, 1e-12, AOT.mult, AOT.max))
    vs.wait(v)
    vs.tag(v.tensor_copy(s_ap, rs_t[:, 0:NB]))
    vs.wait(v)
    vs.tag(v.reciprocal(rs_t[:, 0:NB], rs_t[:, 0:NB]))
    vs.wait(v)
    vs.tag(v.tensor_tensor(blk(b1), blk(src[:, 0:W]), sv(rs_t[:, 0:NB]), AOT.mult))
    vs.wait(v)
    vs.tag(v.tensor_scalar(b2, b1, MH, MH, AOT.add, AOT.subtract))
    vs.tag(v.tensor_scalar(b3, b1, MO, MO, AOT.add, AOT.subtract))
    vs.wait(v)
    vs.tag(v.tensor_scalar(b2, b2, 2.0, -2.0, AOT.min, AOT.max))
    vs.tag(v.tensor_scalar(b4, b3, 2.0, -2.0, AOT.min, AOT.max))
    vs.tag(v.tensor_scalar(b3, b3, 4.0, -4.0, AOT.min, AOT.max))
    vs.wait(v)
    vs.tag(v.tensor_tensor(b3, b3, b4, AOT.subtract))
    vs.wait(v)
    vs.tag(v.tensor_tensor(b2, b2, b3, AOT.add))
    vs.tag(v.tensor_scalar(b4, b1, 5.0, 2.0, AOT.is_gt, AOT.mult))
    vs.wait(v)
    vs.tag(v.tensor_scalar(b1, b1, -5.0, 2.0, AOT.is_lt, AOT.mult))
    vs.wait(v)
    vs.tag(v.tensor_tensor(b4, b4, b1, AOT.subtract))
    vs.wait(v)
    vs.tag(v.tensor_tensor(b2, b2, b4, AOT.add))
    vs.wait(v)
    vs.tag(v.tensor_tensor(blk(out_bf[:, 0:W]), blk(b2), sv(s_ap), AOT.mult))
    vs.wait(v)


def emit_rsqrt(v, out, var_eps, t1, t2, vs, iters=2):
    vs.tag(v.tensor_scalar(t1.bitcast(I32), var_eps.bitcast(I32), 1, None,
                           AOT.arith_shift_right))
    vs.wait(v)
    vs.tag(v.tensor_scalar(t1.bitcast(I32), t1.bitcast(I32), 0, None, AOT.bitwise_not))
    vs.wait(v)
    vs.tag(v.tensor_scalar(t1.bitcast(I32), t1.bitcast(I32), RSQRT_MAGIC + 1, None, AOT.add))
    vs.wait(v)
    vs.tag(v.tensor_scalar(t2, var_eps, -0.5, None, AOT.mult))
    vs.wait(v)
    for _ in range(iters):
        vs.tag(v.tensor_tensor(out, t1, t1, AOT.mult))
        vs.wait(v)
        vs.tag(v.tensor_tensor(out, out, t2, AOT.mult))
        vs.wait(v)
        vs.tag(v.tensor_scalar(out, out, 1.5, None, AOT.add))
        vs.wait(v)
        vs.tag(v.tensor_tensor(t1, t1, out, AOT.mult))
        vs.wait(v)
    vs.tag(v.tensor_copy(out, t1))
    vs.wait(v)


def emit_ln_smalls(v, vs, ln, E):
    vs.tag(v.scalar_tensor_tensor(ln[:, 7:8], ln[:, 0:1], 1.0 / (E * E), ln[:, 0:1],
                                  AOT.mult, AOT.mult))
    vs.wait(v)
    vs.tag(v.scalar_tensor_tensor(ln[:, 2:3], ln[:, 1:2], 1.0 / E, ln[:, 7:8],
                                  AOT.mult, AOT.subtract))
    vs.wait(v)
    vs.tag(v.tensor_scalar(ln[:, 2:3], ln[:, 2:3], 1e-5, None, AOT.add))
    vs.wait(v)


def build_program(cfg):
    S, E, H, D, I = cfg['S'], cfg['E'], cfg['H'], cfg['D'], cfg['I']
    EC, ST = E // 128, S // 128
    ICT = I // 128
    KPR = E // D
    RPH = S * D // E
    TPH = RPH // 128
    assert RPH * H == S and KPR * D == E and TPH >= 1
    GUC = []
    off = 0
    while off < I:
        GUC.append((off, min(1024, I - off)))
        off += 1024
    NG = len(GUC)
    NQX = QUANT_TAGS * len(quant_calls(E))
    BV1 = LN_TAGS + 1
    QT1 = QUANT_TAGS + 1
    VT = BV1 + NQX + QT1 * NG + LN_TAGS
    SCT = 1 + EC + NG + ICT + 1
    TT_ = 1 + EC + NG + ICT + 1
    NTR = EC + ICT
    AVT = 18
    AST = 6 + EC
    ATT = 3 + EC
    scale = 1.0 / (D ** 0.5)

    nc = bass.Bass('TRN2', target_bir_lowering=False, debug=False)

    xT_h = nc.dram_tensor('xT', [E, S], F32, kind='ExternalInput').ap()
    x_h = nc.dram_tensor('x', [S, E], F32, kind='ExternalInput').ap()
    WT_h = {w: nc.dram_tensor('W%sT' % w, [E, E], F32, kind='ExternalInput').ap()
            for w in ('q', 'k', 'v', 'o')}
    b_h = {w: nc.dram_tensor('b%s' % w, [E], F32, kind='ExternalInput').ap()
           for w in ('q', 'k', 'v', 'o')}
    wg_h = nc.dram_tensor('wg', [I, E], F32, kind='ExternalInput').ap()
    wu_h = nc.dram_tensor('wu', [I, E], F32, kind='ExternalInput').ap()
    wd_h = nc.dram_tensor('wd', [E, I], F32, kind='ExternalInput').ap()
    ident_h = nc.dram_tensor('ident', [128, 128], F32, kind='ExternalInput').ap()
    out_h = nc.dram_tensor('out', [S, E], F32, kind='ExternalOutput').ap()

    wgq_scr = nc.dram_tensor('wgq_scr', [EC, 128, I], BF16, kind='Internal').ap()
    wuq_scr = nc.dram_tensor('wuq_scr', [EC, 128, I], BF16, kind='Internal').ap()
    wdq_scr = nc.dram_tensor('wdq_scr', [ICT, 128, E], BF16, kind='Internal').ap()
    aT_scr = nc.dram_tensor('aT_scr', [E, S], BF16, kind='Internal').ap()

    ident = nc.alloc_sbuf_tensor('ident_sb', [128, 128], BF16)
    ones = nc.alloc_sbuf_tensor('ones_sb', [1, 128], BF16)
    browB = nc.alloc_sbuf_tensor('browB_sb', [1, E], BF16)
    WoT = nc.alloc_sbuf_tensor('WoT_sb', [128, EC, E], BF16)

    gdma = Sem(nc, 'gdma')
    vs = Sem(nc, 'vs')
    ss = Sem(nc, 'ss')
    tsm = Sem(nc, 'tsm')

    with nc.Block() as blk:
        @blk.gpsimd
        def _(g):
            gdma.tag(g.dma_start(out=ident[:], in_=ident_h), 16)
            gdma.tag(g.dma_start(out=browB[0:1, :], in_=b_h['o'].unsqueeze(0)), 16)
            gdma.tag(g.dma_start(out=WoT[:], in_=WT_h['o'].rearrange('(c p) f -> p c f', p=128)), 16)

        @blk.vector
        def _(v):
            vs.tag(v.memset(ones[:], 1.0))
            vs.wait(v)

    QWW = 1024

    def wcalls(W):
        calls, off = [], 0
        while off < W:
            calls.append((off, min(QWW, W - off)))
            off += QWW
        return calls

    jobs = []
    for src, scr in ((wg_h, wgq_scr), (wu_h, wuq_scr)):
        for rc in range(I // 128):
            jobs.append((src[rc * 128:(rc + 1) * 128, :], E, scr, rc))
    for rc in range(EC):
        jobs.append((wd_h[rc * 128:(rc + 1) * 128, :], I, wdq_scr, rc))
    NJ = len(jobs)
    ntrs = [W // 128 for _, W, _, _ in jobs]
    callseq = [(j, off, cw) for j in range(NJ) for (off, cw) in wcalls(jobs[j][1])]
    NCALL = len(callseq)
    ST_ = cfg['S'] // 128
    NPH = 2 * (ST_ - 1)
    for cap in range(max(1, -(-NCALL // (NPH - 2))), 9):
        cphase = [0] * NCALL
        lc_ph, fc_ph = {}, {}
        ph, used, ci = 2, 0, 0
        ok = True
        for j in range(NJ):
            nc_j = len(wcalls(jobs[j][1]))
            if j >= 2 and ph < lc_ph[j - 2] + 1:
                ph, used = lc_ph[j - 2] + 1, 0
            for _ in range(nc_j):
                if used >= cap:
                    ph, used = ph + 1, 0
                if ph >= NPH:
                    ok = False
                    break
                cphase[ci] = ph
                lc_ph[j] = ph
                fc_ph.setdefault(j, ph)
                used += 1
                ci += 1
            if not ok:
                break
        if ok:
            break
    assert ok, ('schedule overflow', NCALL, NPH)
    ldjobs = {}
    for j in range(NJ):
        ldjobs.setdefault(max(0, fc_ph[j] // 2 - 1), []).append(j)
    slotcalls = {}
    for c in range(NCALL):
        slotcalls.setdefault((cphase[c] // 2, cphase[c] % 2), []).append(c)
    v_after, t_after, m_start = [], [], []
    cumv = cumt = 0
    for j in range(NJ):
        cumv += QUANT_TAGS * len(wcalls(jobs[j][1]))
        v_after.append(cumv)
        m_start.append(cumt)
        cumt += ntrs[j]
        t_after.append(cumt)
    ttop, tmid, epre, epost, spre, spost = {}, {}, {}, {}, {}, {}
    for j in range(NJ):
        n, p = lc_ph[j] // 2, lc_ph[j] % 2
        if p == 0:
            tmid.setdefault(n, []).append(j)
            epost.setdefault(n, []).append(j)
            spost.setdefault(n, []).append(j)
        else:
            ttop.setdefault(n + 1, []).append(j)
            epre.setdefault(n + 1, []).append(j)
            spre.setdefault(n + 1, []).append(j)
    GS = 4
    ngr = [(ntrs[j] + GS - 1) // GS for j in range(NJ)]
    g_start, g_after = [], []
    cg = 0
    for j in range(NJ):
        g_start.append(cg)
        cg += ngr[j]
        g_after.append(cg)
    wdm = [Sem(nc, 'wdm0'), Sem(nc, 'wdm1')]
    psp = [Sem(nc, 'psp0'), Sem(nc, 'psp1')]
    vq = Sem(nc, 'vq')
    tq = Sem(nc, 'tq')
    sq = Sem(nc, 'sq')
    pq = Sem(nc, 'pq')

    gA, tA, sA, vA = gdma.n, tsm.n, ss.n, vs.n
    sA2 = sA + 4
    def sscast(n):
        return sA + 1 if n == 0 else sA2 + (n - 1) * AST + 2
    def ssqkv(n, wi):
        return sA + 2 + wi if n == 0 else sA2 + (n - 1) * AST + 3 + wi
    def ssaob(n):
        return sA2 + n * AST + (6 if n < ST - 1 else 2)
    def ssatev(n, c):
        return sA2 + n * AST + (7 if n < ST - 1 else 3) + c
    tA2 = tA + 3
    def tsproj(n, wi):
        return tA + wi + 1 if n == 0 else tA2 + (n - 1) * ATT + wi + 1
    def tsattn(n, c):
        return tA2 + n * ATT + (4 if n < ST - 1 else 1) + c
    with contextlib.ExitStack() as es:
        ec_ = es.enter_context
        browA = ec_(nc.sbuf_tensor('browA_sb', [1, 3 * E], BF16))
        Wq_sb = ec_(nc.sbuf_tensor('Wq_sb', [128, EC, E], BF16))
        Wk_sb = ec_(nc.sbuf_tensor('Wk_sb', [128, EC, E], BF16))
        Wv_sb = ec_(nc.sbuf_tensor('Wv_sb', [128, EC, E], BF16))
        xTs = ec_(nc.sbuf_tensor('xTs', [128, EC, 128], F32))
        xTb = ec_(nc.sbuf_tensor('xTb', [128, EC, 128], BF16))
        Qt = ec_(nc.sbuf_tensor('Qt', [128, 2, E], BF16))
        Kt = ec_(nc.sbuf_tensor('Kt', [128, 2, E], BF16))
        Vt = ec_(nc.sbuf_tensor('Vt', [128, 2, E], BF16))
        VT2 = ec_(nc.sbuf_tensor('VT2', [128, E], BF16))
        prod = ec_(nc.sbuf_tensor('prod', [128, 4, H * H * D // 4], BF16))
        prod2b = ec_(nc.sbuf_tensor('prod2b', [128, H * H * D // 4], BF16))
        scf = ec_(nc.sbuf_tensor('scf', [128, H * H], F32))
        mx = ec_(nc.sbuf_tensor('mx', [128, H], F32))
        scm = ec_(nc.sbuf_tensor('scm', [128, H * H], F32))
        expf = ec_(nc.sbuf_tensor('expf', [128, H * H], F32))
        sums = ec_(nc.sbuf_tensor('sums', [128, H], F32))
        rec = ec_(nc.sbuf_tensor('rec', [128, H], F32))
        Amat = ec_(nc.sbuf_tensor('Amat', [128, H * H], BF16))
        ao = ec_(nc.sbuf_tensor('ao', [128, E], F32))
        aob = ec_(nc.sbuf_tensor('aob', [128, E], BF16))
        aTstage = ec_(nc.sbuf_tensor('aTstage', [128, EC, 128], BF16))
        pP0 = ec_(nc.psum_tensor('pP0', [128, E], F32))
        pP1 = ec_(nc.psum_tensor('pP1', [128, E], F32))
        ptA0 = ec_(nc.psum_tensor('ptA0', [128, 128], BF16))
        ptA1 = ec_(nc.psum_tensor('ptA1', [128, 128], BF16))
        wstage = ec_(nc.sbuf_tensor('wstage', [128, 2, I], F32))
        wqbf = ec_(nc.sbuf_tensor('wqbf', [128, 2, I], BF16))
        pstage = ec_(nc.sbuf_tensor('pstage', [128, 2, ICT, 128], BF16))
        qb0w = ec_(nc.sbuf_tensor('qb0', [128, QWW], F32))
        qb1w = ec_(nc.sbuf_tensor('qb1', [128, QWW], F32))
        qb2w = ec_(nc.sbuf_tensor('qb2', [128, QWW], F32))
        qb3w = ec_(nc.sbuf_tensor('qb3', [128, QWW], F32))
        qsw = ec_(nc.sbuf_tensor('qs_sb0', [128, QWW // 16], F32))
        qrsw = ec_(nc.sbuf_tensor('qrs_sb0', [128, QWW // 16], F32))
        ptrW = (ec_(nc.psum_tensor('ptrw0', [128, GS, 128], BF16)),
                ec_(nc.psum_tensor('ptrw1', [128, GS, 128], BF16)))
        blk = ec_(nc.Block())
        ptA = (ptA0, ptA1)
        xdm = [Sem(nc, 'axd0'), Sem(nc, 'axd1')]
        asp = [Sem(nc, 'asp0'), Sem(nc, 'asp1')]
        pP = (pP0, pP1)
        pqkv = ((Wq_sb, Qt), (Wk_sb, Kt), (Wv_sb, Vt))

        @blk.gpsimd
        def _(g):
            for wi, w in enumerate(('q', 'k', 'v')):
                if wi >= 1:
                    g.wait_ge(gdma.h, gA + 32 * wi)
                gdma.tag(g.dma_start(out=(Wq_sb, Wk_sb, Wv_sb)[wi][:],
                                     in_=WT_h[w].rearrange('(c p) f -> p c f', p=128)), 16)
                gdma.tag(g.dma_start(out=browA[0:1, wi * E:(wi + 1) * E],
                                     in_=b_h[w].unsqueeze(0)), 16)

            HQ = H // 4
            for n in range(ST):
                base_s = sA + n * AST
                base_v = vA + n * AVT
                g.wait_ge(ss.h, ssqkv(n, 1))
                for k in range(4):
                    if n >= 1:
                        g.wait_ge(vs.h, vA + (n - 1) * AVT + 1 + k)
                    qv = Qt[:, n % 2, k * 256:k * 256 + 256]\
                        .rearrange('p (i d) -> p i d', d=D)\
                        .unsqueeze(2).broadcast_to([128, HQ, H, D])
                    kv = Kt[:, n % 2, :].rearrange('p (j d) -> p j d', d=D)\
                        .unsqueeze(1).broadcast_to([128, HQ, H, D])
                    pq.tag(g.tensor_tensor(
                        prod[:, k, :].rearrange('p (i j d) -> p i j d', j=H, d=D),
                        qv, kv, AOT.mult))

        @blk.sync
        def _(sy):
            def spill(jj):
                scr, rc = jobs[jj][2], jobs[jj][3]
                sy.wait_ge(sq.h, g_after[jj])
                psp[jj % 2].tag(sy.dma_start(
                    out=scr[:, :, rc * 128:(rc + 1) * 128].rearrange('c p i -> p c i'),
                    in_=pstage[:, jj % 2, 0:ntrs[jj], :]), 16)
            xdm[0].tag(sy.dma_start(
                out=xTs[:], in_=xT_h[:, 0:128].rearrange('(c p) t -> p c t', p=128)), 16)
            for n in range(ST):
                if n + 1 < ST:
                    sy.wait_ge(ss.h, sscast(n))
                    xdm[(n + 1) % 2].tag(sy.dma_start(
                        out=xTs[:],
                        in_=xT_h[:, (n + 1) * 128:(n + 2) * 128]
                        .rearrange('(c p) t -> p c t', p=128)), 16)
                for jj in spre.get(n, []) + spost.get(n, []):
                    spill(jj)
                sy.wait_ge(ss.h, ssatev(n, EC - 1))
                asp[n % 2].tag(sy.dma_start(
                    out=aT_scr[:, n * 128:(n + 1) * 128].rearrange('(c p) t -> p c t', p=128),
                    in_=aTstage[:]), 16)
                for j in ldjobs.get(n, []):
                    if j >= 2:
                        sy.wait_ge(vq.h, v_after[j - 2])
                    wdm[j % 2].tag(sy.dma_start(out=wstage[:, j % 2, 0:jobs[j][1]],
                                                in_=jobs[j][0]), 16)
            asp[0].wait(sy)
            asp[1].wait(sy)
            psp[0].wait(sy)
            psp[1].wait(sy)

        @blk.tensor
        def _(t):
            def wtrans(t, jlist):
                for j in jlist:
                    t.wait_ge(vq.h, v_after[j])
                    for k in range(ntrs[j]):
                        G = g_start[j] + k // GS
                        if k % GS == 0 and G >= 2:
                            t.wait_ge(sq.h, G - 1)
                        tq.tag(t.transpose(ptrW[G % 2][:, k % GS, :],
                                           wqbf[:, j % 2, k * 128:(k + 1) * 128], ident[:]))
            def proj(n):
                t.wait_ge(ss.h, sscast(n))
                for wi, (wsb, dst) in enumerate(pqkv):
                    if n == 0:
                        t.wait_ge(gdma.h, gA + 32 * (wi + 1))
                    q = n * 3 + wi
                    pw = pP[q % 2]
                    if q >= 2:
                        qp = q - 2
                        t.wait_ge(ss.h, ssqkv(qp // 3, qp % 3))
                    ins = None
                    for half in range(2):
                        sl = slice(half * 512, half * 512 + 512)
                        for ecx in range(EC):
                            t.matmul(pw[:, sl], xTb[:, ecx, :], wsb[:, ecx, sl],
                                     start=(ecx == 0), stop=False)
                        ins = t.matmul(pw[:, sl], ones[0:1, :],
                                       browA[0:1, wi * E + half * 512:wi * E + half * 512 + 512],
                                       start=False, stop=True)
                    tsm.tag(ins)
            proj(0)
            for n in range(ST):
                wtrans(t, tmid.get(n, []))
                wtrans(t, ttop.get(n + 1, []))
                if n + 1 < ST:
                    proj(n + 1)
                t.wait_ge(ss.h, ssaob(n))
                for c in range(EC):
                    m = n * EC + c
                    if m >= 2:
                        mm_ = m - 2
                        t.wait_ge(ss.h, ssatev(mm_ // EC, mm_ % EC))
                    tsm.tag(t.transpose(ptA[m % 2][:], aob[:, c * 128:(c + 1) * 128], ident[:]))

        @blk.scalar
        def _(s):
            def wevict(s, jlist):
                for j in jlist:
                    if j >= 2:
                        s.wait_ge(psp[j % 2].h, ((j - 2) // 2 + 1) * 16)
                    for gk in range(ngr[j]):
                        gl = min(GS, ntrs[j] - gk * GS)
                        s.wait_ge(tq.h, m_start[j] + gk * GS + gl)
                        sq.tag(s.copy(pstage[:, j % 2, gk * GS:gk * GS + gl, :],
                                      ptrW[(g_start[j] + gk) % 2][:, 0:gl, :]))
            def cast_evict(n):
                s.wait_ge(xdm[n % 2].h, (n // 2 + 1) * 16)
                if n >= 1:
                    s.wait_ge(tsm.h, tsproj(n - 1, 2))
                ss.tag(s.copy(xTb[:].rearrange('p c t -> p (c t)'),
                              xTs[:].rearrange('p c t -> p (c t)')))
                for wi, (wsb, dst) in enumerate(pqkv):
                    s.wait_ge(tsm.h, tsproj(n, wi))
                    if n >= 2:
                        s.wait_ge(vs.h, vA + (n - 1) * AVT)
                        s.wait_ge(pq.h, (n - 2) * 4 + 4)
                    ss.tag(s.copy(dst[:, n % 2, :], pP[(n * 3 + wi) % 2][:]))
            cast_evict(0)
            for n in range(ST):
                base_t = tA + n * ATT
                base_v = vA + n * AVT
                s.wait_ge(vs.h, base_v + 6)
                if n >= 1:
                    s.wait_ge(vs.h, vA + (n - 1) * AVT + 10)
                ss.tag(s.activation(expf[:], scm[:], AF.Exp, scale=scale))
                wevict(s, epost.get(n, []))
                if n + 1 < ST:
                    wevict(s, epre.get(n + 1, []))
                    cast_evict(n + 1)
                s.wait_ge(vs.h, base_v + 18)
                if n >= 1:
                    s.wait_ge(tsm.h, tsattn(n - 1, EC - 1))
                ss.tag(s.copy(aob[:], ao[:]))
                if n >= 1:
                    s.wait_ge(asp[(n - 1) % 2].h, ((n - 1) // 2 + 1) * 16)
                for c in range(EC):
                    s.wait_ge(tsm.h, tsattn(n, c))
                    ss.tag(s.copy(aTstage[:, c, :], ptA[(n * EC + c) % 2][:]))

        @blk.vector
        def _(v):
            def vquant(c):
                j, off, cw = callseq[c]
                if off == 0:
                    v.wait_ge(wdm[j % 2].h, (j // 2 + 1) * 16)
                    if j >= 2:
                        v.wait_ge(tq.h, t_after[j - 2])
                emit_quant(v, wstage[:, j % 2, off:off + cw], wqbf[:, j % 2, off:off + cw],
                           qb0w[:, 0:cw], qb1w[:, 0:cw], qb2w[:, 0:cw], qb3w[:, 0:cw],
                           qsw[:, 0:cw // 16], qrsw, cw, vq, bf=QBF_W)
            for n in range(ST):
                for k in range(4):
                    v.wait_ge(pq.h, n * 4 + 1 + k)
                    vs.tag(v.reduce_sum(
                        scf[:, k * 64:k * 64 + 64].rearrange('p (i j) -> p i j', j=H),
                        prod[:, k, :].rearrange('p (i j d) -> p i j d', j=H, d=D),
                        axis=AX.X))
                    vs.wait(v)
                vs.tag(v.reduce_max(mx[:], scf[:].rearrange('p (i j) -> p i j', j=H), axis=AX.X))
                vs.wait(v)
                vs.tag(v.tensor_tensor(scm[:].rearrange('p (i j) -> p i j', j=H),
                                       scf[:].rearrange('p (i j) -> p i j', j=H),
                                       mx[:].unsqueeze(2).broadcast_to([128, H, H]),
                                       AOT.subtract))
                vs.wait(v)
                v.wait_ge(ss.h, ssqkv(n, 2))
                vs.tag(v.tensor_copy(VT2[:].rearrange('p (d j) -> p d j', j=H),
                                     Vt[:, n % 2, :].rearrange('p (j d) -> p j d', d=D)
                                     .transpose([0, 2, 1])))
                vs.wait(v)
                for c in slotcalls.get((n, 0), []) + slotcalls.get((n, 1), []):
                    vquant(c)
                v.wait_ge(ss.h, sA2 + n * AST + 1)
                vs.tag(v.reduce_sum(sums[:], expf[:].rearrange('p (i j) -> p i j', j=H),
                                    axis=AX.X))
                vs.wait(v)
                vs.tag(v.reciprocal(rec[:], sums[:]))
                vs.wait(v)
                vs.tag(v.tensor_tensor(Amat[:].rearrange('p (i j) -> p i j', j=H),
                                       expf[:].rearrange('p (i j) -> p i j', j=H),
                                       rec[:].unsqueeze(2).broadcast_to([128, H, H]),
                                       AOT.mult))
                vs.wait(v)
                for k in range(4):
                    av2 = Amat[:, k * 64:k * 64 + 64]\
                        .rearrange('p (i j) -> p i j', j=H)\
                        .unsqueeze(2).broadcast_to([128, H // 4, D, H])
                    vv2 = VT2[:].rearrange('p (d j) -> p d j', j=H)\
                        .unsqueeze(1).broadcast_to([128, H // 4, D, H])
                    vs.tag(v.tensor_tensor(
                        prod2b[:].rearrange('p (i d j) -> p i d j', d=D, j=H),
                        av2, vv2, AOT.mult))
                    vs.wait(v)
                    vs.tag(v.reduce_sum(
                        ao[:, k * 256:k * 256 + 256].rearrange('p (i d) -> p i d', d=D),
                        prod2b[:].rearrange('p (i d j) -> p i d j', d=D, j=H),
                        axis=AX.X))
                    vs.wait(v)

    gB = gdma.n
    M = {}

    HTK = []
    koff = 0
    for gi, (io, iw) in enumerate(GUC):
        HTK.append(list(range(koff, koff + iw // 128)))
        koff += iw // 128
    vplan = []
    vplan.append(('r1', 0))
    vplan.append(('ln1', 0))
    vplan.append(('x1q', 0))
    for st in range(ST):
        for gi in range(NG):
            vplan.append(('mult', st, gi))
            vplan.append(('hq', st, gi))
        if st + 1 < ST:
            vplan.append(('r1', st + 1))
            vplan.append(('ln1', st + 1))
            vplan.append(('x1q', st + 1))
        vplan.append(('r2', st))
        vplan.append(('ln2', st))
    VCOST = {'r1': 2, 'ln1': LN_TAGS - 1, 'x1q': QUANT_TAGS, 'mult': 1,
             'hq': QUANT_TAGS, 'r2': 1, 'ln2': LN_TAGS - 1}
    acc = vs.n
    for it in vplan:
        acc += VCOST[it[0]]
        M[('v',) + it] = acc
    splan = [('ln1apply', 0)] + [('x1qTev', 0, ecx) for ecx in range(EC)]
    for st in range(ST):
        for gi in range(NG):
            splan.append(('silu', st, gi))
            if gi >= 1:
                for k in HTK[gi - 1]:
                    splan.append(('hqTev', st, k))
        for k in HTK[NG - 1]:
            splan.append(('hqTev', st, k))
        if st + 1 < ST:
            splan.append(('ln1apply', st + 1))
            for ecx in range(EC):
                splan.append(('x1qTev', st + 1, ecx))
        splan.append(('ln2apply', st))
    acc = ss.n
    for it in splan:
        acc += 1
        M[('s',) + it] = acc
    tplan = [('wo', 0)] + [('x1qtr', 0, ecx) for ecx in range(EC)] + [('gu', 0, 0)]
    for st in range(ST):
        for gi in range(1, NG):
            tplan.append(('gu', st, gi))
            for k in HTK[gi - 1]:
                tplan.append(('hqtr', st, k))
            tplan.append(('down', st, gi - 1))
        if st + 1 < ST:
            tplan.append(('wo', st + 1))
        for k in HTK[NG - 1]:
            tplan.append(('hqtr', st, k))
        tplan.append(('down', st, NG - 1))
        if st + 1 < ST:
            for ecx in range(EC):
                tplan.append(('x1qtr', st + 1, ecx))
            tplan.append(('gu', st + 1, 0))
    acc = tsm.n
    for it in tplan:
        acc += 1
        M[('t',) + it] = acc

    def qtr_user(m):
        st_, idx = divmod(m, NTR)
        if idx < EC:
            return ('x1qTev', st_, idx), ('x1qtr', st_, idx)
        return ('hqTev', st_, idx - EC), ('hqtr', st_, idx - EC)

    with contextlib.ExitStack() as es:
        ec_ = es.enter_context
        wgqT = ec_(nc.sbuf_tensor('wgqT', [128, EC, I], BF16))
        wuqT = ec_(nc.sbuf_tensor('wuqT', [128, EC, I], BF16))
        wdqT = ec_(nc.sbuf_tensor('wdqT', [128, ICT, E], BF16))
        aTd = ec_(nc.sbuf_tensor('aTd', [128, S], BF16))
        xt_sb = ec_(nc.sbuf_tensor('xt_sb', [128, E], F32))
        ot_sb = ec_(nc.sbuf_tensor('ot_sb', [128, E], F32))
        r1_sb = ec_(nc.sbuf_tensor('r1_sb', [128, E], F32))
        x1_sb = ec_(nc.sbuf_tensor('x1_sb', [128, 2, E], F32))
        x1q_sb = ec_(nc.sbuf_tensor('x1q_sb', [128, E], BF16))
        x1qT_sb = ec_(nc.sbuf_tensor('x1qT_sb', [128, EC, 128], BF16))
        h_sb = ec_(nc.sbuf_tensor('h_sb', [128, 2, 1024], F32))
        hq_sb = ec_(nc.sbuf_tensor('hq_sb', [128, 2, 1024], BF16))
        hqT_sb = ec_(nc.sbuf_tensor('hqT_sb', [128, ICT, 128], BF16))
        ln_sb = ec_(nc.sbuf_tensor('ln_sb', [128, 16], F32))
        qb0 = ec_(nc.sbuf_tensor('qb0b', [128, QW], F32))
        qs = ec_(nc.sbuf_tensor('qs_sbb', [128, QW // 16], F32))
        qrs = ec_(nc.sbuf_tensor('qrs_sbb', [128, QW // 16], F32))
        pgw = ec_(nc.psum_tensor('pgw', [128, 1024], F32))
        pu = ec_(nc.psum_tensor('pu', [128, 1024], F32))
        pd = ec_(nc.psum_tensor('pd', [128, 1024], F32))
        qtr0 = ec_(nc.psum_tensor('qtr0', [128, 128], BF16))
        qtr1 = ec_(nc.psum_tensor('qtr1', [128, 128], BF16))
        blk = ec_(nc.Block())
        qtr = (qtr0, qtr1)
        adm = [Sem(nc, 'badm0'), Sem(nc, 'badm1')]
        bwl = [Sem(nc, 'bwl0'), Sem(nc, 'bwl1')]
        xdm = [Sem(nc, 'bxd0'), Sem(nc, 'bxd1')]
        osp = [Sem(nc, 'osp0'), Sem(nc, 'osp1')]
        r2v = lambda: h_sb[:, 0, 0:E]
        lnv1 = ln_sb[:, 0:8]
        lnv2 = ln_sb[:, 8:16]

        @blk.gpsimd
        def _(g):
            for gi, (io, iw) in enumerate(GUC):
                if gi >= 1:
                    g.wait_ge(bwl[0].h, gi * 16)
                bwl[0].tag(g.dma_start(out=wgqT[:, :, io:io + iw],
                                       in_=wgq_scr[:, :, io:io + iw].rearrange('c p i -> p c i')), 16)
                if gi >= 1:
                    g.wait_ge(bwl[1].h, gi * 16)
                bwl[1].tag(g.dma_start(out=wuqT[:, :, io:io + iw],
                                       in_=wuq_scr[:, :, io:io + iw].rearrange('c p i -> p c i')), 16)
            hw = ICT // 2
            g.wait_ge(bwl[0].h, NG * 16)
            bwl[0].tag(g.dma_start(out=wdqT[:, 0:hw, :],
                                   in_=wdq_scr[0:hw].rearrange('c p i -> p c i')), 16)
            g.wait_ge(bwl[1].h, NG * 16)
            bwl[1].tag(g.dma_start(out=wdqT[:, hw:ICT, :],
                                   in_=wdq_scr[hw:ICT].rearrange('c p i -> p c i')), 16)
            for st in range(ST):
                g.wait_ge(ss.h, M[('s', 'ln2apply', st)])
                osp[st % 2].tag(g.dma_start(out=out_h[st * 128:(st + 1) * 128, :],
                                            in_=ot_sb[:]), 16)
            osp[0].wait(g)
            osp[1].wait(g)

        @blk.sync
        def _(sy):
            xdm[0].tag(sy.dma_start(out=xt_sb[:], in_=x_h[0:128, :]), 16)
            for st in range(ST):
                hh = st // TPH
                if st % TPH == 0:
                    if st >= 1:
                        sy.wait_ge(tsm.h, M[('t', 'wo', st - 1)])
                    adm[hh % 2].tag(sy.dma_start(
                        out=aTd[0:64, :], in_=aT_scr[hh * 64:(hh + 1) * 64, :]), 16)
                    adm[hh % 2].tag(sy.dma_start(
                        out=aTd[64:128, :], in_=aT_scr[hh * 64:(hh + 1) * 64, :]), 16)
                if st + 1 < ST:
                    sy.wait_ge(vs.h, M[('v', 'x1q', 0)] if st == 0
                               else M[('v', 'r2', st - 1)] + 1)
                    xdm[(st + 1) % 2].tag(sy.dma_start(
                        out=xt_sb[:], in_=x_h[(st + 1) * 128:(st + 2) * 128, :]), 16)

        def down_part(t, st, gi):
            if st == 0:
                t.wait_ge(bwl[0].h, (NG + 1) * 16)
                if HTK[gi][-1] >= ICT // 2:
                    t.wait_ge(bwl[1].h, (NG + 1) * 16)
            if gi == 0:
                t.wait_ge(ss.h, M[('s', 'hqTev', st, HTK[0][-1])])
                if st >= 1:
                    t.wait_ge(vs.h, M[('v', 'r2', st - 1)])
            else:
                t.wait_ge(ss.h, M[('s', 'hqTev', st, HTK[gi][-1])])
            for k in HTK[gi]:
                for hhx in range(2):
                    fs = slice(hhx * 512, hhx * 512 + 512)
                    ins = t.matmul(pd[:, fs], hqT_sb[:, k, :], wdqT[:, k, fs],
                                   start=(k == 0), stop=(k == ICT - 1))
            tsm.tag(ins)

        @blk.tensor
        def _(t):
            def wo_mm(st):
                hh = st // TPH
                mbase = (st % TPH) * 128
                t.wait_ge(adm[hh % 2].h, (hh // 2 + 1) * 32)
                if st >= 1:
                    t.wait_ge(ss.h, M[('s', 'silu', st - 1, NG - 1)])
                    t.wait_ge(vs.h, M[('v', 'mult', st - 1, NG - 1)])
                for k in range(KPR):
                    par = k % 2
                    dstp = pgw if par == 0 else pu
                    lhsT = aTd[par * 64:par * 64 + 64, :]\
                        .rearrange('p (m k) -> p k m', k=KPR)[:, k, mbase:mbase + 128]
                    for fc in range(2):
                        fs = slice(fc * 512, fc * 512 + 512)
                        t.matmul(dstp[:, fs], lhsT,
                                 WoT[par * 64:par * 64 + 64, k // 2, fs],
                                 start=(k < 2),
                                 stop=(par == 1 and k == KPR - 1),
                                 tile_position=(par * 64, 0))
                for fc in range(2):
                    fs = slice(fc * 512, fc * 512 + 512)
                    ins = t.matmul(pgw[:, fs], ones[0:1, :],
                                   browB[0:1, fc * 512:fc * 512 + 512],
                                   start=False, stop=True)
                tsm.tag(ins)

            def x1qtr(st):
                t.wait_ge(vs.h, M[('v', 'x1q', st)])
                for ecx in range(EC):
                    m = st * NTR + ecx
                    if m >= 2:
                        t.wait_ge(ss.h, M[('s',) + qtr_user(m - 2)[0]])
                    tsm.tag(t.transpose(qtr[m % 2][:], x1q_sb[:, ecx * 128:(ecx + 1) * 128],
                                        ident[:]))
            def gu_mm(st, gi):
                io, iw = GUC[gi]
                if st <= 1:
                    t.wait_ge(bwl[0].h, (gi + 1) * 16)
                    t.wait_ge(bwl[1].h, (gi + 1) * 16)
                if gi == 0:
                    t.wait_ge(ss.h, M[('s', 'x1qTev', st, EC - 1)])
                    t.wait_ge(vs.h, M[('v', 'r1', st)])
                else:
                    t.wait_ge(vs.h, M[('v', 'mult', st, gi - 1)])
                    t.wait_ge(ss.h, M[('s', 'silu', st, gi - 1)])
                nho = (iw + 511) // 512
                for ecx in range(EC):
                    for hhx in range(nho):
                        w0 = hhx * 512
                        w1 = min(iw, w0 + 512)
                        t.matmul(pgw[:, w0:w1], x1qT_sb[:, ecx, :],
                                 wgqT[:, ecx, io + w0:io + w1],
                                 start=(ecx == 0), stop=(ecx == EC - 1))
                        ins = t.matmul(pu[:, w0:w1], x1qT_sb[:, ecx, :],
                                       wuqT[:, ecx, io + w0:io + w1],
                                       start=(ecx == 0), stop=(ecx == EC - 1))
                tsm.tag(ins)
            wo_mm(0)
            x1qtr(0)
            gu_mm(0, 0)
            for st in range(ST):
                for gi in range(1, NG):
                    gu_mm(st, gi)
                    if gi >= 1:
                        t.wait_ge(vs.h, M[('v', 'hq', st, gi - 1)])
                        for k in HTK[gi - 1]:
                            m = st * NTR + EC + k
                            if m >= 2:
                                t.wait_ge(ss.h, M[('s',) + qtr_user(m - 2)[0]])
                            kl = k - HTK[gi - 1][0]
                            tsm.tag(t.transpose(qtr[m % 2][:],
                                                hq_sb[:, (gi - 1) % 2, kl * 128:(kl + 1) * 128],
                                                ident[:]))
                        down_part(t, st, gi - 1)
                if st + 1 < ST:
                    wo_mm(st + 1)
                t.wait_ge(vs.h, M[('v', 'hq', st, NG - 1)])
                for k in HTK[NG - 1]:
                    m = st * NTR + EC + k
                    if m >= 2:
                        t.wait_ge(ss.h, M[('s',) + qtr_user(m - 2)[0]])
                    kl = k - HTK[NG - 1][0]
                    tsm.tag(t.transpose(qtr[m % 2][:],
                                        hq_sb[:, (NG - 1) % 2, kl * 128:(kl + 1) * 128],
                                        ident[:]))
                down_part(t, st, NG - 1)
                if st + 1 < ST:
                    x1qtr(st + 1)
                    gu_mm(st + 1, 0)

        @blk.scalar
        def _(s):
            def ln1apply(st):
                s.wait_ge(vs.h, M[('v', 'ln1', st)])
                if st >= 2:
                    s.wait_ge(vs.h, M[('v', 'r2', st - 2)])
                ss.tag(s.activation(x1_sb[:, st % 2, :], r1_sb[:], AF.Identity,
                                    bias=lnv1[:, 3:4], scale=lnv1[:, 4:5]))

            def x1qtev(st):
                for ecx in range(EC):
                    s.wait_ge(tsm.h, M[('t', 'x1qtr', st, ecx)])
                    ss.tag(s.copy(x1qT_sb[:, ecx, :], qtr[(st * NTR + ecx) % 2][:]))
            ln1apply(0)
            x1qtev(0)
            for st in range(ST):
                def hqtev(klist):
                    for k in klist:
                        s.wait_ge(tsm.h, M[('t', 'hqtr', st, k)])
                        ss.tag(s.copy(hqT_sb[:, k, :], qtr[(st * NTR + EC + k) % 2][:]))
                for gi, (io, iw) in enumerate(GUC):
                    s.wait_ge(tsm.h, M[('t', 'gu', st, gi)])
                    if gi >= 2:
                        s.wait_ge(vs.h, M[('v', 'hq', st, gi - 2)])
                    elif st >= 1:
                        s.wait_ge(vs.h, M[('v', 'hq', st - 1, NG - 1 - (1 - gi))])
                    if gi == 0 and st >= 1:
                        s.wait_ge(vs.h, M[('v', 'ln2', st - 1)])
                        s.wait_ge(ss.h, M[('s', 'ln2apply', st - 1)])
                    ss.tag(s.activation(h_sb[:, gi % 2, 0:iw], pgw[:, 0:iw], AF.Silu))
                    if gi >= 1:
                        hqtev(HTK[gi - 1])
                hqtev(HTK[NG - 1])
                if st + 1 < ST:
                    ln1apply(st + 1)
                    x1qtev(st + 1)
                s.wait_ge(vs.h, M[('v', 'ln2', st)])
                if st >= 1:
                    s.wait_ge(osp[(st - 1) % 2].h, ((st - 1) // 2 + 1) * 16)
                ss.tag(s.activation(ot_sb[:], r2v(), AF.Identity,
                                    bias=lnv2[:, 3:4], scale=lnv2[:, 4:5]))

        @blk.vector
        def _(v):
            def front(st):
                v.wait_ge(tsm.h, M[('t', 'wo', st)])
                v.wait_ge(xdm[st % 2].h, (st // 2 + 1) * 16)
                vs.tag(v.scalar_tensor_tensor(r1_sb[:], pgw[:, 0:E], 1.0, xt_sb[:],
                                              AOT.mult, AOT.add))
                vs.wait(v)
                vs.tag(v.scalar_tensor_tensor(r1_sb[:], pu[:, 0:E], 1.0, r1_sb[:],
                                              AOT.mult, AOT.add, accum_out=lnv1[:, 0:1]))
                vs.wait(v)
                assert vs.n == M[('v', 'r1', st)]
                vs.tag(v.scalar_tensor_tensor(x1q_sb[:], r1_sb[:], 1.0, r1_sb[:],
                                              AOT.mult, AOT.mult, accum_out=lnv1[:, 1:2]))
                vs.wait(v)
                emit_ln_smalls(v, vs, lnv1, E)
                emit_rsqrt(v, lnv1[:, 4:5], lnv1[:, 2:3], lnv1[:, 5:6], lnv1[:, 6:7], vs)
                vs.tag(v.scalar_tensor_tensor(lnv1[:, 3:4], lnv1[:, 0:1], -1.0 / E,
                                              lnv1[:, 4:5], AOT.mult, AOT.mult))
                vs.wait(v)
                assert vs.n == M[('v', 'ln1', st)]

            def x1quant(st):
                v.wait_ge(ss.h, M[('s', 'ln1apply', st)])
                emit_quant(v, x1_sb[:, st % 2, :], x1q_sb, qb0[:, 0:E],
                           h_sb[:, 1, 0:E], xt_sb[:, 0:E],
                           r1_sb[:, 0:E], qs[:, 0:E // 16], qrs, E, vs, bf=QBF_X)
                assert vs.n == M[('v', 'x1q', st)]
            front(0)
            x1quant(0)
            for st in range(ST):
                for gi, (io, iw) in enumerate(GUC):
                    v.wait_ge(ss.h, M[('s', 'silu', st, gi)])
                    sl = h_sb[:, gi % 2, 0:iw]
                    vs.tag(v.tensor_tensor(sl, sl, pu[:, 0:iw], AOT.mult))
                    vs.wait(v)
                    if gi >= 2:
                        v.wait_ge(tsm.h, M[('t', 'hqtr', st, HTK[gi - 2][-1])])
                    elif st >= 1:
                        v.wait_ge(tsm.h, M[('t', 'hqtr', st - 1, HTK[NG - 2 + gi][-1])])
                    qbv0 = qb0.bitcast(BF16)
                    qbv1 = r1_sb.bitcast(BF16)
                    emit_quant(v, sl, hq_sb[:, gi % 2, 0:iw],
                               qbv0[:, 0:iw], qbv0[:, 1024:1024 + iw],
                               qbv1[:, 0:iw], qbv1[:, 1024:1024 + iw],
                               qs.bitcast(BF16)[:, 0:iw // 16], qrs, iw, vs, bf=QBF_H)
                    assert vs.n == M[('v', 'hq', st, gi)]
                if st + 1 < ST:
                    front(st + 1)
                    x1quant(st + 1)
                v.wait_ge(tsm.h, M[('t', 'down', st, NG - 1)])
                vs.tag(v.scalar_tensor_tensor(r2v(), pd[:, 0:E], 1.0, x1_sb[:, st % 2, :],
                                              AOT.mult, AOT.add, accum_out=lnv2[:, 0:1]))
                vs.wait(v)
                assert vs.n == M[('v', 'r2', st)]
                vs.tag(v.scalar_tensor_tensor(xt_sb[:], r2v(), 1.0, r2v(),
                                              AOT.mult, AOT.mult, accum_out=lnv2[:, 1:2]))
                vs.wait(v)
                emit_ln_smalls(v, vs, lnv2, E)
                emit_rsqrt(v, lnv2[:, 4:5], lnv2[:, 2:3], lnv2[:, 5:6], lnv2[:, 6:7], vs)
                vs.tag(v.scalar_tensor_tensor(lnv2[:, 3:4], lnv2[:, 0:1], -1.0 / E,
                                              lnv2[:, 4:5], AOT.mult, AOT.mult))
                vs.wait(v)
                assert vs.n == M[('v', 'ln2', st)]

    nc.finalize()
    return nc


_NC_CACHE = {}


def kernel(L0_attn_x, L0_attn_Wq, L0_attn_bq, L0_attn_Wk, L0_attn_bk,
           L0_attn_Wv, L0_attn_bv, L0_attn_Wo, L0_attn_bo,
           L0_ffn_w_gate, L0_ffn_w_up, L0_ffn_w_down):
    cfg = full_cfg()
    x = np.asarray(L0_attn_x, dtype=np.float32)
    N = x.shape[0]
    common = {
        'WqT': np.ascontiguousarray(np.asarray(L0_attn_Wq, np.float32).T),
        'WkT': np.ascontiguousarray(np.asarray(L0_attn_Wk, np.float32).T),
        'WvT': np.ascontiguousarray(np.asarray(L0_attn_Wv, np.float32).T),
        'WoT': np.ascontiguousarray(np.asarray(L0_attn_Wo, np.float32).T),
        'bq': np.asarray(L0_attn_bq, np.float32),
        'bk': np.asarray(L0_attn_bk, np.float32),
        'bv': np.asarray(L0_attn_bv, np.float32),
        'bo': np.asarray(L0_attn_bo, np.float32),
        'wg': np.ascontiguousarray(np.asarray(L0_ffn_w_gate, np.float32)),
        'wu': np.ascontiguousarray(np.asarray(L0_ffn_w_up, np.float32)),
        'wd': np.ascontiguousarray(np.asarray(L0_ffn_w_down, np.float32)),
        'ident': np.eye(128, dtype=np.float32),
    }
    if 'full' not in _NC_CACHE:
        _NC_CACHE['full'] = build_program(cfg)
    nc = _NC_CACHE['full']
    in_maps = []
    for n in range(N):
        m = dict(common)
        m['x'] = np.ascontiguousarray(x[n])
        m['xT'] = np.ascontiguousarray(x[n].T)
        in_maps.append(m)
    res = run_bass_kernel_spmd(nc, in_maps, core_ids=list(range(N)))
    return np.stack([res.results[n]['out'] for n in range(N)], axis=0)


# version 25 (final)
# speedup vs baseline: 1.0138x; 1.0138x over previous
import contextlib
import numpy as np
import concourse.bass as bass
import concourse.mybir as mybir
from concourse.bass_utils import run_bass_kernel_spmd

F32 = mybir.dt.float32
BF16 = mybir.dt.bfloat16
I32 = mybir.dt.int32
AOT = mybir.AluOpType
AF = mybir.ActivationFunctionType
AX = mybir.AxisListType

MAGIC_HALF = 96.0
MAGIC_ONE = 192.0
RSQRT_MAGIC = 0x5f3759df
QW = 1024

QUANT_TAGS = 17
QBF_W = False
QBF_X = False
QBF_H = True
RSQRT_TAGS = 13
LN_TAGS = 2 + 3 + RSQRT_TAGS + 1


def full_cfg():
    return dict(S=4096, E=1024, H=16, D=64, I=2816)


class Sem:
    def __init__(self, nc, name):
        self.h = nc.alloc_semaphore(name)
        self.n = 0

    def tag(self, instr, k=1):
        instr.then_inc(self.h, k)
        self.n += k
        return self.n

    def wait(self, eng, v=None):
        eng.wait_ge(self.h, self.n if v is None else v)


def quant_calls(W):
    calls = []
    off = 0
    while off < W:
        calls.append((off, min(QW, W - off)))
        off += QW
    return calls


def emit_quant(v, src, out_bf, b1, b2, b3, b4, s_ap, rs_t, W, vs, bf=False):
    NB = W // 16
    MH, MO = (96.0, 192.0) if bf else (6291456.0, 12582912.0)
    sv = lambda t: t.unsqueeze(2).broadcast_to([128, NB, 16])
    blk = lambda t: t.rearrange('p (nb b) -> p nb b', b=16)
    vs.tag(v.reduce_max(rs_t[:, 0:NB], blk(src[:, 0:W]), axis=AX.X, apply_absolute_value=True))
    vs.wait(v)
    vs.tag(v.tensor_scalar(rs_t[:, 0:NB], rs_t[:, 0:NB], 1.0 / 6.0, 1e-12, AOT.mult, AOT.max))
    vs.wait(v)
    vs.tag(v.tensor_copy(s_ap, rs_t[:, 0:NB]))
    vs.wait(v)
    vs.tag(v.reciprocal(rs_t[:, 0:NB], rs_t[:, 0:NB]))
    vs.wait(v)
    vs.tag(v.tensor_tensor(blk(b1), blk(src[:, 0:W]), sv(rs_t[:, 0:NB]), AOT.mult))
    vs.wait(v)
    vs.tag(v.tensor_scalar(b2, b1, MH, MH, AOT.add, AOT.subtract))
    vs.tag(v.tensor_scalar(b3, b1, MO, MO, AOT.add, AOT.subtract))
    vs.wait(v)
    vs.tag(v.tensor_scalar(b2, b2, 2.0, -2.0, AOT.min, AOT.max))
    vs.tag(v.tensor_scalar(b4, b3, 2.0, -2.0, AOT.min, AOT.max))
    vs.tag(v.tensor_scalar(b3, b3, 4.0, -4.0, AOT.min, AOT.max))
    vs.wait(v)
    vs.tag(v.tensor_tensor(b3, b3, b4, AOT.subtract))
    vs.wait(v)
    vs.tag(v.tensor_tensor(b2, b2, b3, AOT.add))
    vs.tag(v.tensor_scalar(b4, b1, 5.0, 2.0, AOT.is_gt, AOT.mult))
    vs.wait(v)
    vs.tag(v.tensor_scalar(b1, b1, -5.0, 2.0, AOT.is_lt, AOT.mult))
    vs.wait(v)
    vs.tag(v.tensor_tensor(b4, b4, b1, AOT.subtract))
    vs.wait(v)
    vs.tag(v.tensor_tensor(b2, b2, b4, AOT.add))
    vs.wait(v)
    vs.tag(v.tensor_tensor(blk(out_bf[:, 0:W]), blk(b2), sv(s_ap), AOT.mult))
    vs.wait(v)


def emit_rsqrt(v, out, var_eps, t1, t2, vs, iters=2):
    vs.tag(v.tensor_scalar(t1.bitcast(I32), var_eps.bitcast(I32), 1, None,
                           AOT.arith_shift_right))
    vs.wait(v)
    vs.tag(v.tensor_scalar(t1.bitcast(I32), t1.bitcast(I32), 0, None, AOT.bitwise_not))
    vs.wait(v)
    vs.tag(v.tensor_scalar(t1.bitcast(I32), t1.bitcast(I32), RSQRT_MAGIC + 1, None, AOT.add))
    vs.wait(v)
    vs.tag(v.tensor_scalar(t2, var_eps, -0.5, None, AOT.mult))
    vs.wait(v)
    for _ in range(iters):
        vs.tag(v.tensor_tensor(out, t1, t1, AOT.mult))
        vs.wait(v)
        vs.tag(v.tensor_tensor(out, out, t2, AOT.mult))
        vs.wait(v)
        vs.tag(v.tensor_scalar(out, out, 1.5, None, AOT.add))
        vs.wait(v)
        vs.tag(v.tensor_tensor(t1, t1, out, AOT.mult))
        vs.wait(v)
    vs.tag(v.tensor_copy(out, t1))
    vs.wait(v)


def emit_ln_smalls(v, vs, ln, E):
    vs.tag(v.scalar_tensor_tensor(ln[:, 7:8], ln[:, 0:1], 1.0 / (E * E), ln[:, 0:1],
                                  AOT.mult, AOT.mult))
    vs.wait(v)
    vs.tag(v.scalar_tensor_tensor(ln[:, 2:3], ln[:, 1:2], 1.0 / E, ln[:, 7:8],
                                  AOT.mult, AOT.subtract))
    vs.wait(v)
    vs.tag(v.tensor_scalar(ln[:, 2:3], ln[:, 2:3], 1e-5, None, AOT.add))
    vs.wait(v)


def build_program(cfg):
    S, E, H, D, I = cfg['S'], cfg['E'], cfg['H'], cfg['D'], cfg['I']
    EC, ST = E // 128, S // 128
    ICT = I // 128
    KPR = E // D
    RPH = S * D // E
    TPH = RPH // 128
    assert RPH * H == S and KPR * D == E and TPH >= 1
    GUC = []
    off = 0
    while off < I:
        GUC.append((off, min(1024, I - off)))
        off += 1024
    NG = len(GUC)
    NQX = QUANT_TAGS * len(quant_calls(E))
    BV1 = LN_TAGS + 1
    QT1 = QUANT_TAGS + 1
    VT = BV1 + NQX + QT1 * NG + LN_TAGS
    SCT = 1 + EC + NG + ICT + 1
    TT_ = 1 + EC + NG + ICT + 1
    NTR = EC + ICT
    AVT = 18
    AST = 6 + EC
    ATT = 3 + EC
    scale = 1.0 / (D ** 0.5)

    nc = bass.Bass('TRN2', target_bir_lowering=False, debug=False)

    xT_h = nc.dram_tensor('xT', [E, S], F32, kind='ExternalInput').ap()
    x_h = nc.dram_tensor('x', [S, E], F32, kind='ExternalInput').ap()
    WT_h = {w: nc.dram_tensor('W%sT' % w, [E, E], F32, kind='ExternalInput').ap()
            for w in ('q', 'k', 'v', 'o')}
    b_h = {w: nc.dram_tensor('b%s' % w, [E], F32, kind='ExternalInput').ap()
           for w in ('q', 'k', 'v', 'o')}
    wg_h = nc.dram_tensor('wg', [I, E], F32, kind='ExternalInput').ap()
    wu_h = nc.dram_tensor('wu', [I, E], F32, kind='ExternalInput').ap()
    wd_h = nc.dram_tensor('wd', [E, I], F32, kind='ExternalInput').ap()
    ident_h = nc.dram_tensor('ident', [128, 128], F32, kind='ExternalInput').ap()
    out_h = nc.dram_tensor('out', [S, E], F32, kind='ExternalOutput').ap()

    wgq_scr = nc.dram_tensor('wgq_scr', [EC, 128, I], BF16, kind='Internal').ap()
    wuq_scr = nc.dram_tensor('wuq_scr', [EC, 128, I], BF16, kind='Internal').ap()
    wdq_scr = nc.dram_tensor('wdq_scr', [ICT, 128, E], BF16, kind='Internal').ap()
    aT_scr = nc.dram_tensor('aT_scr', [E, S], BF16, kind='Internal').ap()

    ident = nc.alloc_sbuf_tensor('ident_sb', [128, 128], BF16)
    ones = nc.alloc_sbuf_tensor('ones_sb', [1, 128], BF16)
    browB = nc.alloc_sbuf_tensor('browB_sb', [1, E], BF16)
    WoT = nc.alloc_sbuf_tensor('WoT_sb', [128, EC, E], BF16)

    gdma = Sem(nc, 'gdma')
    vs = Sem(nc, 'vs')
    ss = Sem(nc, 'ss')
    tsm = Sem(nc, 'tsm')

    with nc.Block() as blk:
        @blk.gpsimd
        def _(g):
            gdma.tag(g.dma_start(out=ident[:], in_=ident_h), 16)
            gdma.tag(g.dma_start(out=browB[0:1, :], in_=b_h['o'].unsqueeze(0)), 16)
            gdma.tag(g.dma_start(out=WoT[:], in_=WT_h['o'].rearrange('(c p) f -> p c f', p=128)), 16)

        @blk.vector
        def _(v):
            vs.tag(v.memset(ones[:], 1.0))
            vs.wait(v)

    QWW = 1408

    def wcalls(W):
        calls, off = [], 0
        while off < W:
            calls.append((off, min(QWW, W - off)))
            off += QWW
        return calls

    jobs = []
    for src, scr in ((wg_h, wgq_scr), (wu_h, wuq_scr)):
        for rc in range(I // 128):
            jobs.append((src[rc * 128:(rc + 1) * 128, :], E, scr, rc))
    for rc in range(EC):
        jobs.append((wd_h[rc * 128:(rc + 1) * 128, :], I, wdq_scr, rc))
    NJ = len(jobs)
    ntrs = [W // 128 for _, W, _, _ in jobs]
    callseq = [(j, off, cw) for j in range(NJ) for (off, cw) in wcalls(jobs[j][1])]
    NCALL = len(callseq)
    ST_ = cfg['S'] // 128
    NPH = 2 * (ST_ - 1)
    for cap in range(max(1, -(-NCALL // (NPH - 2))), 9):
        cphase = [0] * NCALL
        lc_ph, fc_ph = {}, {}
        ph, used, ci = 2, 0, 0
        ok = True
        for j in range(NJ):
            nc_j = len(wcalls(jobs[j][1]))
            if j >= 2 and ph < lc_ph[j - 2] + 1:
                ph, used = lc_ph[j - 2] + 1, 0
            for _ in range(nc_j):
                if used >= cap:
                    ph, used = ph + 1, 0
                if ph >= NPH:
                    ok = False
                    break
                cphase[ci] = ph
                lc_ph[j] = ph
                fc_ph.setdefault(j, ph)
                used += 1
                ci += 1
            if not ok:
                break
        if ok:
            break
    assert ok, ('schedule overflow', NCALL, NPH)
    ldjobs = {}
    for j in range(NJ):
        ldjobs.setdefault(max(0, fc_ph[j] // 2 - 1), []).append(j)
    slotcalls = {}
    for c in range(NCALL):
        slotcalls.setdefault((cphase[c] // 2, cphase[c] % 2), []).append(c)
    v_after, t_after, m_start = [], [], []
    cumv = cumt = 0
    for j in range(NJ):
        cumv += QUANT_TAGS * len(wcalls(jobs[j][1]))
        v_after.append(cumv)
        m_start.append(cumt)
        cumt += ntrs[j]
        t_after.append(cumt)
    ttop, tmid, epre, epost, spre, spost = {}, {}, {}, {}, {}, {}
    for j in range(NJ):
        n, p = lc_ph[j] // 2, lc_ph[j] % 2
        if p == 0:
            tmid.setdefault(n, []).append(j)
            epost.setdefault(n, []).append(j)
            spost.setdefault(n, []).append(j)
        else:
            ttop.setdefault(n + 1, []).append(j)
            epre.setdefault(n + 1, []).append(j)
            spre.setdefault(n + 1, []).append(j)
    GS = 4
    ngr = [(ntrs[j] + GS - 1) // GS for j in range(NJ)]
    g_start, g_after = [], []
    cg = 0
    for j in range(NJ):
        g_start.append(cg)
        cg += ngr[j]
        g_after.append(cg)
    wdm = [Sem(nc, 'wdm0'), Sem(nc, 'wdm1')]
    psp = [Sem(nc, 'psp0'), Sem(nc, 'psp1')]
    vq = Sem(nc, 'vq')
    tq = Sem(nc, 'tq')
    sq = Sem(nc, 'sq')
    pq = Sem(nc, 'pq')

    gA, tA, sA, vA = gdma.n, tsm.n, ss.n, vs.n
    sA2 = sA + 4
    def sscast(n):
        return sA + 1 if n == 0 else sA2 + (n - 1) * AST + 2
    def ssqkv(n, wi):
        return sA + 2 + wi if n == 0 else sA2 + (n - 1) * AST + 3 + wi
    def ssaob(n):
        return sA2 + n * AST + (6 if n < ST - 1 else 2)
    def ssatev(n, c):
        return sA2 + n * AST + (7 if n < ST - 1 else 3) + c
    tA2 = tA + 3
    def tsproj(n, wi):
        return tA + wi + 1 if n == 0 else tA2 + (n - 1) * ATT + wi + 1
    def tsattn(n, c):
        return tA2 + n * ATT + (4 if n < ST - 1 else 1) + c
    with contextlib.ExitStack() as es:
        ec_ = es.enter_context
        browA = ec_(nc.sbuf_tensor('browA_sb', [1, 3 * E], BF16))
        Wq_sb = ec_(nc.sbuf_tensor('Wq_sb', [128, EC, E], BF16))
        Wk_sb = ec_(nc.sbuf_tensor('Wk_sb', [128, EC, E], BF16))
        Wv_sb = ec_(nc.sbuf_tensor('Wv_sb', [128, EC, E], BF16))
        xTs = ec_(nc.sbuf_tensor('xTs', [128, EC, 128], F32))
        xTb = ec_(nc.sbuf_tensor('xTb', [128, EC, 128], BF16))
        Qt = ec_(nc.sbuf_tensor('Qt', [128, 2, E], BF16))
        Kt = ec_(nc.sbuf_tensor('Kt', [128, 2, E], BF16))
        Vt = ec_(nc.sbuf_tensor('Vt', [128, E], BF16))
        VT2 = ec_(nc.sbuf_tensor('VT2', [128, E], BF16))
        prod = ec_(nc.sbuf_tensor('prod', [128, 4, H * H * D // 4], BF16))
        prod2b = ec_(nc.sbuf_tensor('prod2b', [128, H * H * D // 4], BF16))
        scf = ec_(nc.sbuf_tensor('scf', [128, H * H], F32))
        mx = ec_(nc.sbuf_tensor('mx', [128, H], F32))
        sums = ec_(nc.sbuf_tensor('sums', [128, H], F32))
        rec = ec_(nc.sbuf_tensor('rec', [128, H], F32))
        Amat = ec_(nc.sbuf_tensor('Amat', [128, H * H], BF16))
        ao = ec_(nc.sbuf_tensor('ao', [128, E], F32))
        aob = ec_(nc.sbuf_tensor('aob', [128, E], BF16))
        aTstage = ec_(nc.sbuf_tensor('aTstage', [128, EC, 128], BF16))
        pP0 = ec_(nc.psum_tensor('pP0', [128, E], F32))
        pP1 = ec_(nc.psum_tensor('pP1', [128, E], F32))
        ptA0 = ec_(nc.psum_tensor('ptA0', [128, 128], BF16))
        ptA1 = ec_(nc.psum_tensor('ptA1', [128, 128], BF16))
        wstage = ec_(nc.sbuf_tensor('wstage', [128, 2, I], F32))
        wqbf = ec_(nc.sbuf_tensor('wqbf', [128, 2, I], BF16))
        pstage = ec_(nc.sbuf_tensor('pstage', [128, 2, ICT, 128], BF16))
        qb0w = ec_(nc.sbuf_tensor('qb0', [128, QWW], F32))
        qb1w = ec_(nc.sbuf_tensor('qb1', [128, QWW], F32))
        qb2w = ec_(nc.sbuf_tensor('qb2', [128, QWW], F32))
        qb3w = ec_(nc.sbuf_tensor('qb3', [128, QWW], F32))
        qsw = ec_(nc.sbuf_tensor('qs_sb0', [128, QWW // 16], F32))
        qrsw = ec_(nc.sbuf_tensor('qrs_sb0', [128, QWW // 16], F32))
        ptrW = (ec_(nc.psum_tensor('ptrw0', [128, GS, 128], BF16)),
                ec_(nc.psum_tensor('ptrw1', [128, GS, 128], BF16)))
        blk = ec_(nc.Block())
        ptA = (ptA0, ptA1)
        xdm = [Sem(nc, 'axd0'), Sem(nc, 'axd1')]
        asp = [Sem(nc, 'asp0'), Sem(nc, 'asp1')]
        pP = (pP0, pP1)
        pqkv = ((Wq_sb, Qt), (Wk_sb, Kt), (Wv_sb, Vt))

        @blk.gpsimd
        def _(g):
            for wi, w in enumerate(('q', 'k', 'v')):
                if wi >= 1:
                    g.wait_ge(gdma.h, gA + 32 * wi)
                gdma.tag(g.dma_start(out=(Wq_sb, Wk_sb, Wv_sb)[wi][:],
                                     in_=WT_h[w].rearrange('(c p) f -> p c f', p=128)), 16)
                gdma.tag(g.dma_start(out=browA[0:1, wi * E:(wi + 1) * E],
                                     in_=b_h[w].unsqueeze(0)), 16)

            HQ = H // 4
            for n in range(ST):
                base_s = sA + n * AST
                base_v = vA + n * AVT
                g.wait_ge(ss.h, ssqkv(n, 1))
                for k in range(4):
                    if n >= 1:
                        g.wait_ge(vs.h, vA + (n - 1) * AVT + 1 + k)
                    qv = Qt[:, n % 2, k * 256:k * 256 + 256]\
                        .rearrange('p (i d) -> p i d', d=D)\
                        .unsqueeze(2).broadcast_to([128, HQ, H, D])
                    kv = Kt[:, n % 2, :].rearrange('p (j d) -> p j d', d=D)\
                        .unsqueeze(1).broadcast_to([128, HQ, H, D])
                    pq.tag(g.tensor_tensor(
                        prod[:, k, :].rearrange('p (i j d) -> p i j d', j=H, d=D),
                        qv, kv, AOT.mult))

        @blk.sync
        def _(sy):
            def spill(jj):
                scr, rc = jobs[jj][2], jobs[jj][3]
                sy.wait_ge(sq.h, g_after[jj])
                psp[jj % 2].tag(sy.dma_start(
                    out=scr[:, :, rc * 128:(rc + 1) * 128].rearrange('c p i -> p c i'),
                    in_=pstage[:, jj % 2, 0:ntrs[jj], :]), 16)
            xdm[0].tag(sy.dma_start(
                out=xTs[:], in_=xT_h[:, 0:128].rearrange('(c p) t -> p c t', p=128)), 16)
            for n in range(ST):
                if n + 1 < ST:
                    sy.wait_ge(ss.h, sscast(n))
                    xdm[(n + 1) % 2].tag(sy.dma_start(
                        out=xTs[:],
                        in_=xT_h[:, (n + 1) * 128:(n + 2) * 128]
                        .rearrange('(c p) t -> p c t', p=128)), 16)
                for jj in spre.get(n, []) + spost.get(n, []):
                    spill(jj)
                sy.wait_ge(ss.h, ssatev(n, EC - 1))
                asp[n % 2].tag(sy.dma_start(
                    out=aT_scr[:, n * 128:(n + 1) * 128].rearrange('(c p) t -> p c t', p=128),
                    in_=aTstage[:]), 16)
                for j in ldjobs.get(n, []):
                    if j >= 2:
                        sy.wait_ge(vq.h, v_after[j - 2])
                    wdm[j % 2].tag(sy.dma_start(out=wstage[:, j % 2, 0:jobs[j][1]],
                                                in_=jobs[j][0]), 16)
            asp[0].wait(sy)
            asp[1].wait(sy)
            psp[0].wait(sy)
            psp[1].wait(sy)

        @blk.tensor
        def _(t):
            def wtrans(t, jlist):
                for j in jlist:
                    t.wait_ge(vq.h, v_after[j])
                    for k in range(ntrs[j]):
                        G = g_start[j] + k // GS
                        if k % GS == 0 and G >= 2:
                            t.wait_ge(sq.h, G - 1)
                        tq.tag(t.transpose(ptrW[G % 2][:, k % GS, :],
                                           wqbf[:, j % 2, k * 128:(k + 1) * 128], ident[:]))
            def proj(n):
                t.wait_ge(ss.h, sscast(n))
                for wi, (wsb, dst) in enumerate(pqkv):
                    if n == 0:
                        t.wait_ge(gdma.h, gA + 32 * (wi + 1))
                    q = n * 3 + wi
                    pw = pP[q % 2]
                    if q >= 2:
                        qp = q - 2
                        t.wait_ge(ss.h, ssqkv(qp // 3, qp % 3))
                    ins = None
                    for half in range(2):
                        sl = slice(half * 512, half * 512 + 512)
                        for ecx in range(EC):
                            t.matmul(pw[:, sl], xTb[:, ecx, :], wsb[:, ecx, sl],
                                     start=(ecx == 0), stop=False)
                        ins = t.matmul(pw[:, sl], ones[0:1, :],
                                       browA[0:1, wi * E + half * 512:wi * E + half * 512 + 512],
                                       start=False, stop=True)
                    tsm.tag(ins)
            proj(0)
            for n in range(ST):
                wtrans(t, tmid.get(n, []))
                wtrans(t, ttop.get(n + 1, []))
                if n + 1 < ST:
                    proj(n + 1)
                t.wait_ge(ss.h, ssaob(n))
                for c in range(EC):
                    m = n * EC + c
                    if m >= 2:
                        mm_ = m - 2
                        t.wait_ge(ss.h, ssatev(mm_ // EC, mm_ % EC))
                    tsm.tag(t.transpose(ptA[m % 2][:], aob[:, c * 128:(c + 1) * 128], ident[:]))

        @blk.scalar
        def _(s):
            def wevict(s, jlist):
                for j in jlist:
                    if j >= 2:
                        s.wait_ge(psp[j % 2].h, ((j - 2) // 2 + 1) * 16)
                    for gk in range(ngr[j]):
                        gl = min(GS, ntrs[j] - gk * GS)
                        s.wait_ge(tq.h, m_start[j] + gk * GS + gl)
                        sq.tag(s.copy(pstage[:, j % 2, gk * GS:gk * GS + gl, :],
                                      ptrW[(g_start[j] + gk) % 2][:, 0:gl, :]))
            def cast_evict(n):
                s.wait_ge(xdm[n % 2].h, (n // 2 + 1) * 16)
                if n >= 1:
                    s.wait_ge(tsm.h, tsproj(n - 1, 2))
                ss.tag(s.copy(xTb[:].rearrange('p c t -> p (c t)'),
                              xTs[:].rearrange('p c t -> p (c t)')))
                for wi, (wsb, dst) in enumerate(pqkv):
                    s.wait_ge(tsm.h, tsproj(n, wi))
                    if n >= 2:
                        s.wait_ge(vs.h, vA + (n - 1) * AVT)
                        s.wait_ge(pq.h, (n - 2) * 4 + 4)
                    if wi == 2:
                        if n >= 1:
                            s.wait_ge(vs.h, vA + (n - 1) * AVT + 7)
                        ss.tag(s.copy(Vt[:], pP[(n * 3 + wi) % 2][:]))
                    else:
                        ss.tag(s.copy(dst[:, n % 2, :], pP[(n * 3 + wi) % 2][:]))
            cast_evict(0)
            for n in range(ST):
                base_t = tA + n * ATT
                base_v = vA + n * AVT
                s.wait_ge(vs.h, base_v + 6)
                if n >= 1:
                    s.wait_ge(vs.h, vA + (n - 1) * AVT + 10)
                ss.tag(s.activation(scf[:], scf[:], AF.Exp, scale=scale))
                wevict(s, epost.get(n, []))
                if n + 1 < ST:
                    wevict(s, epre.get(n + 1, []))
                    cast_evict(n + 1)
                s.wait_ge(vs.h, base_v + 18)
                if n >= 1:
                    s.wait_ge(tsm.h, tsattn(n - 1, EC - 1))
                ss.tag(s.copy(aob[:], ao[:]))
                if n >= 1:
                    s.wait_ge(asp[(n - 1) % 2].h, ((n - 1) // 2 + 1) * 16)
                for c in range(EC):
                    s.wait_ge(tsm.h, tsattn(n, c))
                    ss.tag(s.copy(aTstage[:, c, :], ptA[(n * EC + c) % 2][:]))

        @blk.vector
        def _(v):
            def vquant(c):
                j, off, cw = callseq[c]
                if off == 0:
                    v.wait_ge(wdm[j % 2].h, (j // 2 + 1) * 16)
                    if j >= 2:
                        v.wait_ge(tq.h, t_after[j - 2])
                emit_quant(v, wstage[:, j % 2, off:off + cw], wqbf[:, j % 2, off:off + cw],
                           qb0w[:, 0:cw], qb1w[:, 0:cw], qb2w[:, 0:cw], qb3w[:, 0:cw],
                           qsw[:, 0:cw // 16], qrsw, cw, vq, bf=QBF_W)
            for n in range(ST):
                for k in range(4):
                    v.wait_ge(pq.h, n * 4 + 1 + k)
                    vs.tag(v.reduce_sum(
                        scf[:, k * 64:k * 64 + 64].rearrange('p (i j) -> p i j', j=H),
                        prod[:, k, :].rearrange('p (i j d) -> p i j d', j=H, d=D),
                        axis=AX.X))
                    vs.wait(v)
                vs.tag(v.reduce_max(mx[:], scf[:].rearrange('p (i j) -> p i j', j=H), axis=AX.X))
                vs.wait(v)
                vs.tag(v.tensor_tensor(scf[:].rearrange('p (i j) -> p i j', j=H),
                                       scf[:].rearrange('p (i j) -> p i j', j=H),
                                       mx[:].unsqueeze(2).broadcast_to([128, H, H]),
                                       AOT.subtract))
                vs.wait(v)
                v.wait_ge(ss.h, ssqkv(n, 2))
                vs.tag(v.tensor_copy(VT2[:].rearrange('p (d j) -> p d j', j=H),
                                     Vt[:].rearrange('p (j d) -> p j d', d=D)
                                     .transpose([0, 2, 1])))
                vs.wait(v)
                for c in slotcalls.get((n, 0), []) + slotcalls.get((n, 1), []):
                    vquant(c)
                v.wait_ge(ss.h, sA2 + n * AST + 1)
                vs.tag(v.reduce_sum(sums[:], scf[:].rearrange('p (i j) -> p i j', j=H),
                                    axis=AX.X))
                vs.wait(v)
                vs.tag(v.reciprocal(rec[:], sums[:]))
                vs.wait(v)
                vs.tag(v.tensor_tensor(Amat[:].rearrange('p (i j) -> p i j', j=H),
                                       scf[:].rearrange('p (i j) -> p i j', j=H),
                                       rec[:].unsqueeze(2).broadcast_to([128, H, H]),
                                       AOT.mult))
                vs.wait(v)
                for k in range(4):
                    av2 = Amat[:, k * 64:k * 64 + 64]\
                        .rearrange('p (i j) -> p i j', j=H)\
                        .unsqueeze(2).broadcast_to([128, H // 4, D, H])
                    vv2 = VT2[:].rearrange('p (d j) -> p d j', j=H)\
                        .unsqueeze(1).broadcast_to([128, H // 4, D, H])
                    vs.tag(v.tensor_tensor(
                        prod2b[:].rearrange('p (i d j) -> p i d j', d=D, j=H),
                        av2, vv2, AOT.mult))
                    vs.wait(v)
                    vs.tag(v.reduce_sum(
                        ao[:, k * 256:k * 256 + 256].rearrange('p (i d) -> p i d', d=D),
                        prod2b[:].rearrange('p (i d j) -> p i d j', d=D, j=H),
                        axis=AX.X))
                    vs.wait(v)

    gB = gdma.n
    M = {}

    HTK = []
    koff = 0
    for gi, (io, iw) in enumerate(GUC):
        HTK.append(list(range(koff, koff + iw // 128)))
        koff += iw // 128
    vplan = []
    vplan.append(('r1', 0))
    vplan.append(('ln1', 0))
    vplan.append(('x1q', 0))
    for st in range(ST):
        for gi in range(NG):
            vplan.append(('mult', st, gi))
            vplan.append(('hq', st, gi))
        if st + 1 < ST:
            vplan.append(('r1', st + 1))
            vplan.append(('ln1', st + 1))
            vplan.append(('x1q', st + 1))
        vplan.append(('r2', st))
        vplan.append(('ln2', st))
    VCOST = {'r1': 2, 'ln1': LN_TAGS - 1, 'x1q': QUANT_TAGS, 'mult': 1,
             'hq': QUANT_TAGS, 'r2': 1, 'ln2': LN_TAGS - 1}
    acc = vs.n
    for it in vplan:
        acc += VCOST[it[0]]
        M[('v',) + it] = acc
    splan = [('ln1apply', 0)] + [('x1qTev', 0, ecx) for ecx in range(EC)]
    for st in range(ST):
        for gi in range(NG):
            splan.append(('silu', st, gi))
            if gi >= 1:
                for k in HTK[gi - 1]:
                    splan.append(('hqTev', st, k))
        for k in HTK[NG - 1]:
            splan.append(('hqTev', st, k))
        if st + 1 < ST:
            splan.append(('ln1apply', st + 1))
            for ecx in range(EC):
                splan.append(('x1qTev', st + 1, ecx))
        splan.append(('ln2apply', st))
    acc = ss.n
    for it in splan:
        acc += 1
        M[('s',) + it] = acc
    tplan = [('wo', 0)] + [('x1qtr', 0, ecx) for ecx in range(EC)] + [('gu', 0, 0)]
    for st in range(ST):
        for gi in range(1, NG):
            tplan.append(('gu', st, gi))
            for k in HTK[gi - 1]:
                tplan.append(('hqtr', st, k))
            tplan.append(('down', st, gi - 1))
        if st + 1 < ST:
            tplan.append(('wo', st + 1))
        for k in HTK[NG - 1]:
            tplan.append(('hqtr', st, k))
        tplan.append(('down', st, NG - 1))
        if st + 1 < ST:
            for ecx in range(EC):
                tplan.append(('x1qtr', st + 1, ecx))
            tplan.append(('gu', st + 1, 0))
    acc = tsm.n
    for it in tplan:
        acc += 1
        M[('t',) + it] = acc

    def qtr_user(m):
        st_, idx = divmod(m, NTR)
        if idx < EC:
            return ('x1qTev', st_, idx), ('x1qtr', st_, idx)
        return ('hqTev', st_, idx - EC), ('hqtr', st_, idx - EC)

    with contextlib.ExitStack() as es:
        ec_ = es.enter_context
        wgqT = ec_(nc.sbuf_tensor('wgqT', [128, EC, I], BF16))
        wuqT = ec_(nc.sbuf_tensor('wuqT', [128, EC, I], BF16))
        wdqT = ec_(nc.sbuf_tensor('wdqT', [128, ICT, E], BF16))
        aTd = ec_(nc.sbuf_tensor('aTd', [128, S], BF16))
        xt_sb = ec_(nc.sbuf_tensor('xt_sb', [128, E], F32))
        ot_sb = ec_(nc.sbuf_tensor('ot_sb', [128, E], F32))
        r1_sb = ec_(nc.sbuf_tensor('r1_sb', [128, E], F32))
        x1_sb = ec_(nc.sbuf_tensor('x1_sb', [128, 2, E], F32))
        x1q_sb = ec_(nc.sbuf_tensor('x1q_sb', [128, E], BF16))
        x1qT_sb = ec_(nc.sbuf_tensor('x1qT_sb', [128, EC, 128], BF16))
        h_sb = ec_(nc.sbuf_tensor('h_sb', [128, 2, 1024], F32))
        hq_sb = ec_(nc.sbuf_tensor('hq_sb', [128, 2, 1024], BF16))
        hqT_sb = ec_(nc.sbuf_tensor('hqT_sb', [128, ICT, 128], BF16))
        ln_sb = ec_(nc.sbuf_tensor('ln_sb', [128, 16], F32))
        qb0 = ec_(nc.sbuf_tensor('qb0b', [128, QW], F32))
        qs = ec_(nc.sbuf_tensor('qs_sbb', [128, QW // 16], F32))
        qrs = ec_(nc.sbuf_tensor('qrs_sbb', [128, QW // 16], F32))
        pgw = ec_(nc.psum_tensor('pgw', [128, 1024], F32))
        pu = ec_(nc.psum_tensor('pu', [128, 1024], F32))
        pd = ec_(nc.psum_tensor('pd', [128, 1024], F32))
        qtr0 = ec_(nc.psum_tensor('qtr0', [128, 128], BF16))
        qtr1 = ec_(nc.psum_tensor('qtr1', [128, 128], BF16))
        blk = ec_(nc.Block())
        qtr = (qtr0, qtr1)
        adm = [Sem(nc, 'badm0'), Sem(nc, 'badm1')]
        bwl = [Sem(nc, 'bwl0'), Sem(nc, 'bwl1')]
        xdm = [Sem(nc, 'bxd0'), Sem(nc, 'bxd1')]
        osp = [Sem(nc, 'osp0'), Sem(nc, 'osp1')]
        r2v = lambda: h_sb[:, 0, 0:E]
        lnv1 = ln_sb[:, 0:8]
        lnv2 = ln_sb[:, 8:16]

        @blk.gpsimd
        def _(g):
            for gi, (io, iw) in enumerate(GUC):
                if gi >= 1:
                    g.wait_ge(bwl[0].h, gi * 16)
                bwl[0].tag(g.dma_start(out=wgqT[:, :, io:io + iw],
                                       in_=wgq_scr[:, :, io:io + iw].rearrange('c p i -> p c i')), 16)
                if gi >= 1:
                    g.wait_ge(bwl[1].h, gi * 16)
                bwl[1].tag(g.dma_start(out=wuqT[:, :, io:io + iw],
                                       in_=wuq_scr[:, :, io:io + iw].rearrange('c p i -> p c i')), 16)
            hw = ICT // 2
            g.wait_ge(bwl[0].h, NG * 16)
            bwl[0].tag(g.dma_start(out=wdqT[:, 0:hw, :],
                                   in_=wdq_scr[0:hw].rearrange('c p i -> p c i')), 16)
            g.wait_ge(bwl[1].h, NG * 16)
            bwl[1].tag(g.dma_start(out=wdqT[:, hw:ICT, :],
                                   in_=wdq_scr[hw:ICT].rearrange('c p i -> p c i')), 16)
            for st in range(ST):
                g.wait_ge(ss.h, M[('s', 'ln2apply', st)])
                osp[st % 2].tag(g.dma_start(out=out_h[st * 128:(st + 1) * 128, :],
                                            in_=ot_sb[:]), 16)
            osp[0].wait(g)
            osp[1].wait(g)

        @blk.sync
        def _(sy):
            xdm[0].tag(sy.dma_start(out=xt_sb[:], in_=x_h[0:128, :]), 16)
            for st in range(ST):
                hh = st // TPH
                if st % TPH == 0:
                    if st >= 1:
                        sy.wait_ge(tsm.h, M[('t', 'wo', st - 1)])
                    adm[hh % 2].tag(sy.dma_start(
                        out=aTd[0:64, :], in_=aT_scr[hh * 64:(hh + 1) * 64, :]), 16)
                    adm[hh % 2].tag(sy.dma_start(
                        out=aTd[64:128, :], in_=aT_scr[hh * 64:(hh + 1) * 64, :]), 16)
                if st + 1 < ST:
                    sy.wait_ge(vs.h, M[('v', 'x1q', 0)] if st == 0
                               else M[('v', 'r2', st - 1)] + 1)
                    xdm[(st + 1) % 2].tag(sy.dma_start(
                        out=xt_sb[:], in_=x_h[(st + 1) * 128:(st + 2) * 128, :]), 16)

        def down_part(t, st, gi):
            if st == 0:
                t.wait_ge(bwl[0].h, (NG + 1) * 16)
                if HTK[gi][-1] >= ICT // 2:
                    t.wait_ge(bwl[1].h, (NG + 1) * 16)
            if gi == 0:
                t.wait_ge(ss.h, M[('s', 'hqTev', st, HTK[0][-1])])
                if st >= 1:
                    t.wait_ge(vs.h, M[('v', 'r2', st - 1)])
            else:
                t.wait_ge(ss.h, M[('s', 'hqTev', st, HTK[gi][-1])])
            for k in HTK[gi]:
                for hhx in range(2):
                    fs = slice(hhx * 512, hhx * 512 + 512)
                    ins = t.matmul(pd[:, fs], hqT_sb[:, k, :], wdqT[:, k, fs],
                                   start=(k == 0), stop=(k == ICT - 1))
            tsm.tag(ins)

        @blk.tensor
        def _(t):
            def wo_mm(st):
                hh = st // TPH
                mbase = (st % TPH) * 128
                t.wait_ge(adm[hh % 2].h, (hh // 2 + 1) * 32)
                if st >= 1:
                    t.wait_ge(ss.h, M[('s', 'silu', st - 1, NG - 1)])
                    t.wait_ge(vs.h, M[('v', 'mult', st - 1, NG - 1)])
                for k in range(KPR):
                    par = k % 2
                    dstp = pgw if par == 0 else pu
                    lhsT = aTd[par * 64:par * 64 + 64, :]\
                        .rearrange('p (m k) -> p k m', k=KPR)[:, k, mbase:mbase + 128]
                    for fc in range(2):
                        fs = slice(fc * 512, fc * 512 + 512)
                        t.matmul(dstp[:, fs], lhsT,
                                 WoT[par * 64:par * 64 + 64, k // 2, fs],
                                 start=(k < 2),
                                 stop=(par == 1 and k == KPR - 1),
                                 tile_position=(par * 64, 0))
                for fc in range(2):
                    fs = slice(fc * 512, fc * 512 + 512)
                    ins = t.matmul(pgw[:, fs], ones[0:1, :],
                                   browB[0:1, fc * 512:fc * 512 + 512],
                                   start=False, stop=True)
                tsm.tag(ins)

            def x1qtr(st):
                t.wait_ge(vs.h, M[('v', 'x1q', st)])
                for ecx in range(EC):
                    m = st * NTR + ecx
                    if m >= 2:
                        t.wait_ge(ss.h, M[('s',) + qtr_user(m - 2)[0]])
                    tsm.tag(t.transpose(qtr[m % 2][:], x1q_sb[:, ecx * 128:(ecx + 1) * 128],
                                        ident[:]))
            def gu_mm(st, gi):
                io, iw = GUC[gi]
                if st <= 1:
                    t.wait_ge(bwl[0].h, (gi + 1) * 16)
                    t.wait_ge(bwl[1].h, (gi + 1) * 16)
                if gi == 0:
                    t.wait_ge(ss.h, M[('s', 'x1qTev', st, EC - 1)])
                    t.wait_ge(vs.h, M[('v', 'r1', st)])
                else:
                    t.wait_ge(vs.h, M[('v', 'mult', st, gi - 1)])
                    t.wait_ge(ss.h, M[('s', 'silu', st, gi - 1)])
                nho = (iw + 511) // 512
                for ecx in range(EC):
                    for hhx in range(nho):
                        w0 = hhx * 512
                        w1 = min(iw, w0 + 512)
                        t.matmul(pgw[:, w0:w1], x1qT_sb[:, ecx, :],
                                 wgqT[:, ecx, io + w0:io + w1],
                                 start=(ecx == 0), stop=(ecx == EC - 1))
                        ins = t.matmul(pu[:, w0:w1], x1qT_sb[:, ecx, :],
                                       wuqT[:, ecx, io + w0:io + w1],
                                       start=(ecx == 0), stop=(ecx == EC - 1))
                tsm.tag(ins)
            wo_mm(0)
            x1qtr(0)
            gu_mm(0, 0)
            for st in range(ST):
                for gi in range(1, NG):
                    gu_mm(st, gi)
                    if gi >= 1:
                        t.wait_ge(vs.h, M[('v', 'hq', st, gi - 1)])
                        for k in HTK[gi - 1]:
                            m = st * NTR + EC + k
                            if m >= 2:
                                t.wait_ge(ss.h, M[('s',) + qtr_user(m - 2)[0]])
                            kl = k - HTK[gi - 1][0]
                            tsm.tag(t.transpose(qtr[m % 2][:],
                                                hq_sb[:, (gi - 1) % 2, kl * 128:(kl + 1) * 128],
                                                ident[:]))
                        down_part(t, st, gi - 1)
                if st + 1 < ST:
                    wo_mm(st + 1)
                t.wait_ge(vs.h, M[('v', 'hq', st, NG - 1)])
                for k in HTK[NG - 1]:
                    m = st * NTR + EC + k
                    if m >= 2:
                        t.wait_ge(ss.h, M[('s',) + qtr_user(m - 2)[0]])
                    kl = k - HTK[NG - 1][0]
                    tsm.tag(t.transpose(qtr[m % 2][:],
                                        hq_sb[:, (NG - 1) % 2, kl * 128:(kl + 1) * 128],
                                        ident[:]))
                down_part(t, st, NG - 1)
                if st + 1 < ST:
                    x1qtr(st + 1)
                    gu_mm(st + 1, 0)

        @blk.scalar
        def _(s):
            def ln1apply(st):
                s.wait_ge(vs.h, M[('v', 'ln1', st)])
                if st >= 2:
                    s.wait_ge(vs.h, M[('v', 'r2', st - 2)])
                ss.tag(s.activation(x1_sb[:, st % 2, :], r1_sb[:], AF.Identity,
                                    bias=lnv1[:, 3:4], scale=lnv1[:, 4:5]))

            def x1qtev(st):
                for ecx in range(EC):
                    s.wait_ge(tsm.h, M[('t', 'x1qtr', st, ecx)])
                    ss.tag(s.copy(x1qT_sb[:, ecx, :], qtr[(st * NTR + ecx) % 2][:]))
            ln1apply(0)
            x1qtev(0)
            for st in range(ST):
                def hqtev(klist):
                    for k in klist:
                        s.wait_ge(tsm.h, M[('t', 'hqtr', st, k)])
                        ss.tag(s.copy(hqT_sb[:, k, :], qtr[(st * NTR + EC + k) % 2][:]))
                for gi, (io, iw) in enumerate(GUC):
                    s.wait_ge(tsm.h, M[('t', 'gu', st, gi)])
                    if gi >= 2:
                        s.wait_ge(vs.h, M[('v', 'hq', st, gi - 2)])
                    elif st >= 1:
                        s.wait_ge(vs.h, M[('v', 'hq', st - 1, NG - 1 - (1 - gi))])
                    if gi == 0 and st >= 1:
                        s.wait_ge(vs.h, M[('v', 'ln2', st - 1)])
                        s.wait_ge(ss.h, M[('s', 'ln2apply', st - 1)])
                    ss.tag(s.activation(h_sb[:, gi % 2, 0:iw], pgw[:, 0:iw], AF.Silu))
                    if gi >= 1:
                        hqtev(HTK[gi - 1])
                hqtev(HTK[NG - 1])
                if st + 1 < ST:
                    ln1apply(st + 1)
                    x1qtev(st + 1)
                s.wait_ge(vs.h, M[('v', 'ln2', st)])
                if st >= 1:
                    s.wait_ge(osp[(st - 1) % 2].h, ((st - 1) // 2 + 1) * 16)
                ss.tag(s.activation(ot_sb[:], r2v(), AF.Identity,
                                    bias=lnv2[:, 3:4], scale=lnv2[:, 4:5]))

        @blk.vector
        def _(v):
            def front(st):
                v.wait_ge(tsm.h, M[('t', 'wo', st)])
                v.wait_ge(xdm[st % 2].h, (st // 2 + 1) * 16)
                vs.tag(v.scalar_tensor_tensor(r1_sb[:], pgw[:, 0:E], 1.0, xt_sb[:],
                                              AOT.mult, AOT.add))
                vs.wait(v)
                vs.tag(v.scalar_tensor_tensor(r1_sb[:], pu[:, 0:E], 1.0, r1_sb[:],
                                              AOT.mult, AOT.add, accum_out=lnv1[:, 0:1]))
                vs.wait(v)
                assert vs.n == M[('v', 'r1', st)]
                vs.tag(v.scalar_tensor_tensor(x1q_sb[:], r1_sb[:], 1.0, r1_sb[:],
                                              AOT.mult, AOT.mult, accum_out=lnv1[:, 1:2]))
                vs.wait(v)
                emit_ln_smalls(v, vs, lnv1, E)
                emit_rsqrt(v, lnv1[:, 4:5], lnv1[:, 2:3], lnv1[:, 5:6], lnv1[:, 6:7], vs)
                vs.tag(v.scalar_tensor_tensor(lnv1[:, 3:4], lnv1[:, 0:1], -1.0 / E,
                                              lnv1[:, 4:5], AOT.mult, AOT.mult))
                vs.wait(v)
                assert vs.n == M[('v', 'ln1', st)]

            def x1quant(st):
                v.wait_ge(ss.h, M[('s', 'ln1apply', st)])
                emit_quant(v, x1_sb[:, st % 2, :], x1q_sb, qb0[:, 0:E],
                           h_sb[:, 1, 0:E], xt_sb[:, 0:E],
                           r1_sb[:, 0:E], qs[:, 0:E // 16], qrs, E, vs, bf=QBF_X)
                assert vs.n == M[('v', 'x1q', st)]
            front(0)
            x1quant(0)
            for st in range(ST):
                for gi, (io, iw) in enumerate(GUC):
                    v.wait_ge(ss.h, M[('s', 'silu', st, gi)])
                    sl = h_sb[:, gi % 2, 0:iw]
                    vs.tag(v.tensor_tensor(sl, sl, pu[:, 0:iw], AOT.mult))
                    vs.wait(v)
                    if gi >= 2:
                        v.wait_ge(tsm.h, M[('t', 'hqtr', st, HTK[gi - 2][-1])])
                    elif st >= 1:
                        v.wait_ge(tsm.h, M[('t', 'hqtr', st - 1, HTK[NG - 2 + gi][-1])])
                    qbv0 = qb0.bitcast(BF16)
                    qbv1 = r1_sb.bitcast(BF16)
                    emit_quant(v, sl, hq_sb[:, gi % 2, 0:iw],
                               qbv0[:, 0:iw], qbv0[:, 1024:1024 + iw],
                               qbv1[:, 0:iw], qbv1[:, 1024:1024 + iw],
                               qs.bitcast(BF16)[:, 0:iw // 16], qrs, iw, vs, bf=QBF_H)
                    assert vs.n == M[('v', 'hq', st, gi)]
                if st + 1 < ST:
                    front(st + 1)
                    x1quant(st + 1)
                v.wait_ge(tsm.h, M[('t', 'down', st, NG - 1)])
                vs.tag(v.scalar_tensor_tensor(r2v(), pd[:, 0:E], 1.0, x1_sb[:, st % 2, :],
                                              AOT.mult, AOT.add, accum_out=lnv2[:, 0:1]))
                vs.wait(v)
                assert vs.n == M[('v', 'r2', st)]
                vs.tag(v.scalar_tensor_tensor(xt_sb[:], r2v(), 1.0, r2v(),
                                              AOT.mult, AOT.mult, accum_out=lnv2[:, 1:2]))
                vs.wait(v)
                emit_ln_smalls(v, vs, lnv2, E)
                emit_rsqrt(v, lnv2[:, 4:5], lnv2[:, 2:3], lnv2[:, 5:6], lnv2[:, 6:7], vs)
                vs.tag(v.scalar_tensor_tensor(lnv2[:, 3:4], lnv2[:, 0:1], -1.0 / E,
                                              lnv2[:, 4:5], AOT.mult, AOT.mult))
                vs.wait(v)
                assert vs.n == M[('v', 'ln2', st)]

    nc.finalize()
    return nc


_NC_CACHE = {}


def kernel(L0_attn_x, L0_attn_Wq, L0_attn_bq, L0_attn_Wk, L0_attn_bk,
           L0_attn_Wv, L0_attn_bv, L0_attn_Wo, L0_attn_bo,
           L0_ffn_w_gate, L0_ffn_w_up, L0_ffn_w_down):
    cfg = full_cfg()
    x = np.asarray(L0_attn_x, dtype=np.float32)
    N = x.shape[0]
    common = {
        'WqT': np.ascontiguousarray(np.asarray(L0_attn_Wq, np.float32).T),
        'WkT': np.ascontiguousarray(np.asarray(L0_attn_Wk, np.float32).T),
        'WvT': np.ascontiguousarray(np.asarray(L0_attn_Wv, np.float32).T),
        'WoT': np.ascontiguousarray(np.asarray(L0_attn_Wo, np.float32).T),
        'bq': np.asarray(L0_attn_bq, np.float32),
        'bk': np.asarray(L0_attn_bk, np.float32),
        'bv': np.asarray(L0_attn_bv, np.float32),
        'bo': np.asarray(L0_attn_bo, np.float32),
        'wg': np.ascontiguousarray(np.asarray(L0_ffn_w_gate, np.float32)),
        'wu': np.ascontiguousarray(np.asarray(L0_ffn_w_up, np.float32)),
        'wd': np.ascontiguousarray(np.asarray(L0_ffn_w_down, np.float32)),
        'ident': np.eye(128, dtype=np.float32),
    }
    if 'full' not in _NC_CACHE:
        _NC_CACHE['full'] = build_program(cfg)
    nc = _NC_CACHE['full']
    in_maps = []
    for n in range(N):
        m = dict(common)
        m['x'] = np.ascontiguousarray(x[n])
        m['xT'] = np.ascontiguousarray(x[n].T)
        in_maps.append(m)
    res = run_bass_kernel_spmd(nc, in_maps, core_ids=list(range(N)))
    return np.stack([res.results[n]['out'] for n in range(N)], axis=0)
